# version 03
# baseline (speedup 1.0000x reference)
import numpy as np
from contextlib import ExitStack
import concourse.bass as bass
import concourse.mybir as mybir
from concourse.bass_utils import run_bass_kernel_spmd




F32 = mybir.dt.float32
BF16 = mybir.dt.bfloat16
AF = mybir.ActivationFunctionType
ALU = mybir.AluOpType
AX = mybir.AxisListType

D = 2048
KC = 16
SCALE = float(1.0 / np.sqrt(128.0))
BSHIFT = -8.0


class P:
    def __init__(self, nc):
        self.nc = nc
        self.sems = {}
        self.c = {}

    def sem(self, name):
        if name not in self.sems:
            self.sems[name] = self.nc.alloc_semaphore(name=name)
            self.c[name] = 0
        return self.sems[name]

    def inc(self, inst, name, k):
        self.sem(name)
        inst.then_inc(self.sems[name], k)
        self.c[name] += k
        return self.c[name]

    def bump(self, eng, name, k=1):
        self.sem(name)
        eng.sem_inc(self.sems[name], k)
        self.c[name] += k
        return self.c[name]

    def wait(self, eng, name, val=None):
        self.sem(name)
        eng.wait_ge(self.sems[name], self.c[name] if val is None else val)


def host_chunk_w(wT):
    K, E = wT.shape
    ec = E // 128
    return np.ascontiguousarray(wT.reshape(K, ec, 128).transpose(1, 0, 2)).astype(np.float32, copy=False)


def m01_host():
    n = np.arange(896)[None, :] - 384
    j = np.arange(128)[:, None]
    return (n >= j).astype(np.float32)


def prep_launch2_weights(inp):
    f32 = np.float32
    Wp = np.asarray(inp["L0_attn_Wp"], f32)
    bp = np.asarray(inp["L0_attn_bp"], f32)
    rw = np.asarray(inp["L0_ffn_router_w"], f32)
    rb = np.asarray(inp["L0_ffn_router_b"], f32)
    ew = np.asarray(inp["L0_ffn_expert_w"], f32)
    eb = np.asarray(inp["L0_ffn_expert_b"], f32)
    ow = np.asarray(inp["L0_ffn_out_w"], f32)
    ob = np.asarray(inp["L0_ffn_out_b"], f32)
    wq1 = np.asarray(inp["L1_attn_w_q"], f32)
    wkv = np.asarray(inp["L1_attn_w_kv"], f32)
    d = {}
    d["wp"] = host_chunk_w(Wp.T)
    d["bp"] = np.ascontiguousarray(bp.reshape(16, 128).T)
    d["rw"] = np.ascontiguousarray(rw.reshape(16, 128, 8).transpose(1, 0, 2).reshape(128, 128))
    d["rb"] = np.ascontiguousarray(np.broadcast_to(rb[None, :], (128, 8)))
    d["id128"] = np.eye(128, dtype=f32)
    d["ew"] = np.ascontiguousarray(ew.reshape(8, 2048, 16, 128).transpose(0, 2, 1, 3))
    d["eb"] = np.ascontiguousarray(eb.reshape(8, 16, 128).transpose(2, 0, 1).reshape(128, 128))
    d["ow"] = host_chunk_w(ow)
    d["ob"] = np.ascontiguousarray(ob.reshape(16, 128).T)
    d["wq1"] = host_chunk_w(wq1)
    d["wkv"] = np.ascontiguousarray(wkv.reshape(2048, 8, 128).transpose(1, 0, 2))
    return d


def assemble_ctx0(outs):
    per_core = []
    for c in range(8):
        b, half = c // 2, c % 2
        top = outs[2 * b]["ctxT"][:, half * 2048:(half + 1) * 2048]
        bot = outs[2 * b + 1]["ctxT"][:, half * 2048:(half + 1) * 2048]
        per_core.append(np.ascontiguousarray(np.concatenate([top, bot], axis=0)))
    return per_core


def assemble_l1_attn_inputs(r2, c):
    b, half = c // 2, c % 2
    qA, qB = r2[2 * b]["q1T"], r2[2 * b + 1]["q1T"]
    kvA, kvB = r2[2 * b]["kv1T"], r2[2 * b + 1]["kv1T"]
    ktA, ktB = r2[2 * b]["kvtok"], r2[2 * b + 1]["kvtok"]
    rows = []
    for u in range(8):
        g, pl = u // 2, u % 2
        pp = 2 * half + pl
        r0 = g * 512 + pp * 128
        rows.append(np.concatenate([qA[r0:r0 + 128, :], qB[r0:r0 + 128, :]], axis=1))
    qh_in = np.ascontiguousarray(np.concatenate(rows, axis=0))
    kT_in = np.ascontiguousarray(np.concatenate([kvA[0:512, :], kvB[0:512, :]], axis=1))
    vt_in = np.ascontiguousarray(np.concatenate([ktA[:, 512:1024], ktB[:, 512:1024]], axis=0))
    return {"qh_in": qh_in, "kT_in": kT_in, "vt_in": vt_in}


def assemble_l1_ctx(r3, c):
    b, half = c // 2, c % 2
    oA, oB = r3[2 * b]["o1T"], r3[2 * b + 1]["o1T"]
    cols = slice(half * 2048, (half + 1) * 2048)
    rows = []
    for g in range(4):
        for pp in range(4):
            src = oA if pp < 2 else oB
            u = g * 2 + (pp % 2)
            rows.append(src[u * 128:(u + 1) * 128, cols])
    return np.ascontiguousarray(np.concatenate(rows, axis=0))


def prep_launch4_weights(inp):
    f32 = np.float32
    wd = np.asarray(inp["L1_attn_w_dense"], f32)
    rw1 = np.asarray(inp["L1_ffn_router_w"], f32)
    d = {}
    d["wd"] = host_chunk_w(wd)
    d["zb"] = np.zeros((128, 16), f32)
    d["rw1"] = np.ascontiguousarray(rw1.T.reshape(16, 128, 8).transpose(1, 0, 2).reshape(128, 128))
    return d


def prep_launch5_weights(inp):
    f32 = np.float32
    w1 = np.asarray(inp["L1_ffn_w1"], f32)
    b1 = np.asarray(inp["L1_ffn_b1"], f32)
    w2 = np.asarray(inp["L1_ffn_w2"], f32)
    b2 = np.asarray(inp["L1_ffn_b2"], f32)
    rw1 = np.asarray(inp["L1_ffn_router_w"], f32)
    d = {}
    d["rw1"] = np.ascontiguousarray(rw1.T.reshape(16, 128, 8).transpose(1, 0, 2).reshape(128, 128))
    d["sltI"] = np.triu(np.ones((128, 128), f32), k=1)
    d["w1c"] = np.ascontiguousarray(w1.T.reshape(2048, 64, 128).transpose(1, 0, 2))
    d["b1c"] = np.ascontiguousarray(b1.reshape(64, 128).T)
    d["w2c"] = np.ascontiguousarray(w2.T.reshape(8192, 16, 128).transpose(1, 0, 2))
    d["b2c"] = np.ascontiguousarray(b2.reshape(16, 128).T)
    return d


def wsel_for_core(c):
    w = np.zeros((16, 4), np.float32)
    for cp in range(8):
        if cp < c:
            w[2 * cp, 0] = 1.0
            w[2 * cp + 1, 2] = 1.0
        w[2 * cp, 1] = 1.0
    return w


def counts_matrix(r4):
    G = np.zeros((16, 8), np.float32)
    for c in range(8):
        G[2 * c, :] = r4[c]["cnts"][0, 0:8]
        G[2 * c + 1, :] = r4[c]["cnts"][0, 8:16]
    return G


NH = 8
SQ = 4096
NIC = 8
EC = 16
TT = 4


def ntiles_of(ic):
    return 4 * ic + 4


def tiles_before(h, ic):
    per_head = sum(ntiles_of(i) for i in range(NIC))
    return h * per_head + sum(ntiles_of(i) for i in range(ic))


def build_launch1():
    nc = bass.Bass("TRN2", target_bir_lowering=False, debug=False)
    xT = nc.dram_tensor("xT", [D, D], F32, kind="ExternalInput").ap()
    wq = nc.dram_tensor("wq", [EC, D, 128], F32, kind="ExternalInput").ap()
    wk = nc.dram_tensor("wk", [EC, D, 128], F32, kind="ExternalInput").ap()
    wv = nc.dram_tensor("wv", [D, D], F32, kind="ExternalInput").ap()
    m01 = nc.dram_tensor("m01", [128, 896], F32, kind="ExternalInput").ap()
    ctxT = nc.dram_tensor("ctxT", [1024, SQ], BF16, kind="ExternalOutput").ap()
    yq = nc.dram_tensor("yq", [128, NH * SQ], BF16, kind="Internal").ap()
    yk = nc.dram_tensor("yk", [128, NH * SQ], BF16, kind="Internal").ap()
    yv = nc.dram_tensor("yv", [D, D], BF16, kind="Internal").ap()

    p_total = tiles_before(NH - 1, NIC - 1) + ntiles_of(NIC - 1)

    from contextlib import ExitStack
    es = ExitStack()
    x_bf = es.enter_context(nc.sbuf_tensor("x_bf", [128, KC * D], BF16))
    m01b = es.enter_context(nc.sbuf_tensor("m01b", [128, 896], BF16))
    onesb = es.enter_context(nc.sbuf_tensor("ones", [128, 128], BF16))
    scol = es.enter_context(nc.sbuf_tensor("scol", [128, 1], F32))
    bcol = es.enter_context(nc.sbuf_tensor("bcol", [128, 1], F32))
    ps0 = es.enter_context(nc.psum_tensor("ps0", [128, 512], F32))
    ps1 = es.enter_context(nc.psum_tensor("ps1", [128, 512], F32))
    ps2 = es.enter_context(nc.psum_tensor("ps2", [128, 512], F32))
    ps3 = es.enter_context(nc.psum_tensor("ps3", [128, 512], F32))
    with nc.Block() as block:
        ps = [ps0, ps1, ps2, ps3]
        p = P(nc)

        @block.gpsimd
        def _(g):
            for kc in range(KC):
                p.inc(g.dma_start(out=x_bf[:, kc * D:(kc + 1) * D],
                                  in_=xT[kc * 128:(kc + 1) * 128, :]), "lx", 16)
            p.inc(g.dma_start(out=m01b[:], in_=m01[:]), "lx", 16)

        @block.vector
        def _(v):
            v.memset(onesb[:], 1.0)
            v.memset(scol[:], SCALE)
            p.inc(v.memset(bcol[:], BSHIFT), "vinit", 1)

        es_qk = ExitStack()
        wslab = es_qk.enter_context(nc.sbuf_tensor("wslab", [128, 2, EC * 128], BF16))
        asm = es_qk.enter_context(nc.sbuf_tensor("asm", [128, NH * SQ], BF16))
        if True:
            asm3 = asm.ap().rearrange("p (t c) -> p t c", c=16)
            for wi, (w, ydst) in enumerate([(wq, yq), (wk, yk)]):
                for ec in range(EC):
                    n = wi * EC + ec

                    @block.gpsimd
                    def _(g, n=n, ec=ec, w=w):
                        if n >= 2:
                            p.wait(g, "mm", n - 1)
                        p.inc(g.dma_start(
                            out=wslab[:, n % 2].rearrange("p (kc e) -> p kc e", e=128),
                            in_=w[ec].rearrange("(kc p) e -> p kc e", p=128)), "slab", 16)

                    @block.tensor
                    def _(t, n=n):
                        if n == 0:
                            p.wait(t, "lx")
                        p.wait(t, "slab", (n + 1) * 16)
                        sl = wslab[:, n % 2]
                        for kc in range(KC):
                            t.ldweights(sl[:, kc * 128:(kc + 1) * 128])
                            for tc in range(TT):
                                if kc == 0 and n >= 1:
                                    p.wait(t, "evac", 4 * (n - 1) + tc + 1)
                                mmi = t.matmul(ps[tc][:], sl[:, kc * 128:(kc + 1) * 128],
                                         x_bf[:, kc * D + tc * 512: kc * D + (tc + 1) * 512],
                                         start=(kc == 0), stop=(kc == KC - 1))
                        p.inc(mmi, "mm", 1)

                    @block.vector
                    def _(v, n=n, ec=ec, wi=wi):
                        p.wait(v, "mm", n + 1)
                        if wi == 1 and n == EC:
                            p.wait(v, "ydump", 16)
                        for tc in range(TT):
                            p.inc(v.tensor_copy(asm3[:, tc * 512:(tc + 1) * 512, ec],
                                                ps[tc][:]), "evac", 1)

                @block.sync
                def _(sy, wi=wi, ydst=ydst):
                    p.wait(sy, "evac", 64 * (wi + 1))
                    p.inc(sy.dma_start(out=ydst[:], in_=asm[:]), "ydump", 16)

        es_qk.close()
        es_v = ExitStack()
        wv_sb = es_v.enter_context(nc.sbuf_tensor("wv_sb", [128, KC * D], BF16))
        vasm = es_v.enter_context(nc.sbuf_tensor("vasm", [128, 2, D], BF16))
        if True:
            @block.gpsimd
            def _(g):
                p.wait(g, "ydump", 32)
                for kc in range(KC):
                    p.inc(g.dma_start(out=wv_sb[:, kc * D:(kc + 1) * D],
                                      in_=wv[kc * 128:(kc + 1) * 128, :]), "wvl", 16)

            for tt in range(16):
                @block.tensor
                def _(t, tt=tt):
                    if tt == 0:
                        p.wait(t, "wvl", 256)
                    for kc in range(KC):
                        t.ldweights(x_bf[:, kc * D + tt * 128: kc * D + (tt + 1) * 128])
                        for e4 in range(4):
                            if kc == 0 and tt >= 1:
                                p.wait(t, "evacv", 4 * (tt - 1) + e4 + 1)
                            mmi = t.matmul(ps[e4][:], x_bf[:, kc * D + tt * 128: kc * D + (tt + 1) * 128],
                                     wv_sb[:, kc * D + e4 * 512: kc * D + (e4 + 1) * 512],
                                     start=(kc == 0), stop=(kc == KC - 1))
                    p.inc(mmi, "mmv", 1)

                @block.vector
                def _(v, tt=tt):
                    p.wait(v, "mmv", tt + 1)
                    if tt >= 2:
                        p.wait(v, "vdump", 16 * (tt - 1))
                    for e4 in range(4):
                        p.inc(v.tensor_copy(vasm[:, tt % 2, e4 * 512:(e4 + 1) * 512],
                                            ps[e4][:]), "evacv", 1)

                @block.sync
                def _(sy, tt=tt):
                    p.wait(sy, "evacv", 4 * (tt + 1))
                    p.inc(sy.dma_start(out=yv[tt * 128:(tt + 1) * 128, :], in_=vasm[:, tt % 2]),
                          "vdump", 16)

        es_v.close()
        es_a = ExitStack()
        qh = es_a.enter_context(nc.sbuf_tensor("qh", [128, 2, SQ], BF16))
        kh = es_a.enter_context(nc.sbuf_tensor("kh", [128, 2, SQ], BF16))
        vct = es_a.enter_context(nc.sbuf_tensor("vct", [128, 2, SQ], BF16))
        ptb = es_a.enter_context(nc.sbuf_tensor("ptb", [128, 2, 512], BF16))
        lr = es_a.enter_context(nc.sbuf_tensor("lr", [128, 512], F32))
        ctxa = es_a.enter_context(nc.sbuf_tensor("ctxa", [128, 2, SQ], BF16))
        psS0 = es_a.enter_context(nc.psum_tensor("psS0", [128, 512], F32))
        psS1 = es_a.enter_context(nc.psum_tensor("psS1", [128, 512], F32))
        psO = es_a.enter_context(nc.psum_tensor("psO", [128, 512], F32))
        psL = es_a.enter_context(nc.psum_tensor("psL", [128, 512], F32))
        if True:
            psS = [psS0, psS1]
            for h in range(NH):
                hb = h % 2

                @block.sync
                def _(sy, h=h, hb=hb):
                    if h >= 2:
                        p.wait(sy, "pvl", tiles_before(h - 1, 0))
                    if h == 0:
                        p.wait(sy, "ydump", 32)
                        p.wait(sy, "vdump", 256)
                    p.inc(sy.dma_start(out=qh[:, hb], in_=yq[:, h * SQ:(h + 1) * SQ]), "hload", 16)
                    p.inc(sy.dma_start(out=kh[:, hb], in_=yk[:, h * SQ:(h + 1) * SQ]), "hload", 16)
                    for jt in range(32):
                        src = bass.AP(yv.tensor, (256 * h + 8 * jt) * D,
                                      [[D, 8], [128, 16], [1, 128]])
                        p.inc(sy.dma_start(out=vct[:, hb, jt * 128:(jt + 1) * 128], in_=src),
                              "hload", 16)

                for ic in range(NIC):
                    nt = ntiles_of(ic)
                    base = tiles_before(h, ic)

                    @block.tensor
                    def _(t, h=h, hb=hb, ic=ic, nt=nt, base=base):
                        if ic == 0:
                            p.wait(t, "hload", 16 * 34 * (h + 1))
                            if h == 0:
                                p.wait(t, "vinit", 1)
                        if base > 0:
                            p.wait(t, "norm", h * NIC + ic)
                            p.wait(t, "pexp", base)

                        def mm_s(t, j):
                            t.ldweights(kh[:, hb, j * 128:(j + 1) * 128])
                            p.inc(t.matmul(psS[j % 2][:], kh[:, hb, j * 128:(j + 1) * 128],
                                           qh[:, hb, ic * 512:(ic + 1) * 512],
                                           start=True, stop=True), "smm", 1)

                        mm_s(t, 0)
                        mm_s(t, 1)
                        for j in range(nt):
                            gt = base + j + 1
                            p.wait(t, "pexp", gt)
                            t.ldweights(vct[:, hb, j * 128:(j + 1) * 128])
                            t.matmul(psO[:], vct[:, hb, j * 128:(j + 1) * 128], ptb[:, j % 2],
                                     start=(j == 0), stop=(j == nt - 1))
                            t.ldweights(onesb[:])
                            p.inc(t.matmul(psL[:], onesb[:], ptb[:, j % 2],
                                           start=(j == 0), stop=(j == nt - 1)), "pvl", 1)
                            if j + 2 < nt:
                                mm_s(t, j + 2)


                    @block.scalar
                    def _(se, h=h, ic=ic, nt=nt, base=base):
                        if base > 0:
                            p.wait(se, "pexp", base)
                        for j in range(nt):
                            gt = base + j + 1
                            p.wait(se, "smm", gt)
                            if gt - 2 >= 1:
                                p.wait(se, "pvl", gt - 2)
                            diag = (j >= nt - 4)
                            sem_name = "eraw" if diag else "pexp"
                            p.inc(se.activation(out=ptb[:, j % 2], in_=psS[j % 2][:],
                                                func=AF.Exp, bias=bcol[:, 0:1],
                                                scale=scol[:, 0:1]), sem_name, 1)

                    @block.vector
                    def _(v, h=h, ic=ic, nt=nt, base=base):
                        for r, j in enumerate(range(nt - 4, nt)):
                            gt = base + j + 1
                            p.wait(v, "eraw", (h * NIC + ic) * 4 + r + 1)
                            off = 384 - 128 * (j - (nt - 4))
                            p.inc(v.tensor_mul(ptb[:, j % 2], ptb[:, j % 2],
                                               m01b[:, off:off + 512]), "pexp", 1)
                        p.wait(v, "pvl", base + nt)
                        if h >= 2 and ic == 0:
                            p.wait(v, "cdump", 16 * (h - 1))
                        v.reciprocal(lr[:], psL[:])
                        p.inc(v.tensor_mul(ctxa[:, h % 2, ic * 512:(ic + 1) * 512], psO[:], lr[:]),
                              "norm", 1)

                @block.sync
                def _(sy, h=h):
                    p.wait(sy, "norm", NIC * (h + 1))
                    p.inc(sy.dma_start(out=ctxT[h * 128:(h + 1) * 128, :], in_=ctxa[:, h % 2]),
                          "cdump", 16)

            @block.sync
            def _(sy):
                p.wait(sy, "cdump", 16 * NH)

        es_a.close()
        assert p.c["pexp"] == p_total, (p.c["pexp"], p_total)
    es.close()
    return nc


EC = 16
TT = 4
INV_D = 1.0 / 2048.0
EPS = 1e-5


def build_launch2(dbg=False, BISECT=False):
    nc = bass.Bass("TRN2", target_bir_lowering=False, debug=False)
    ctxT = nc.dram_tensor("ctxT", [D, D], BF16, kind="ExternalInput").ap()
    xT = nc.dram_tensor("xT", [D, D], F32, kind="ExternalInput").ap()
    wp = nc.dram_tensor("wp", [EC, D, 128], F32, kind="ExternalInput").ap()
    bp = nc.dram_tensor("bp", [128, 16], F32, kind="ExternalInput").ap()
    rw = nc.dram_tensor("rw", [128, 128], F32, kind="ExternalInput").ap()
    rb = nc.dram_tensor("rb", [128, 8], F32, kind="ExternalInput").ap()
    id128 = nc.dram_tensor("id128", [128, 128], F32, kind="ExternalInput").ap()
    ew = nc.dram_tensor("ew", [8, EC, D, 128], F32, kind="ExternalInput").ap()
    eb = nc.dram_tensor("eb", [128, 128], F32, kind="ExternalInput").ap()
    ow = nc.dram_tensor("ow", [EC, D, 128], F32, kind="ExternalInput").ap()
    ob = nc.dram_tensor("ob", [128, 16], F32, kind="ExternalInput").ap()
    wq1 = nc.dram_tensor("wq1", [EC, D, 128], F32, kind="ExternalInput").ap()
    wkv = nc.dram_tensor("wkv", [8, D, 128], F32, kind="ExternalInput").ap()
    x2T = nc.dram_tensor("x2T", [D, D], F32, kind="ExternalOutput").ap()
    q1T = nc.dram_tensor("q1T", [D, D], BF16, kind="ExternalOutput").ap()
    kv1T = nc.dram_tensor("kv1T", [1024, D], BF16, kind="ExternalOutput").ap()
    kvtok = nc.dram_tensor("kvtok", [D, 1024], BF16, kind="ExternalOutput").ap()
    ik = "ExternalOutput" if dbg else "Internal"
    x1f = nc.dram_tensor("x1f", [D, D], F32, kind=ik).ap()
    x1b = nc.dram_tensor("x1b", [D, D], BF16, kind="Internal").ap()
    gdram = nc.dram_tensor("gdram", [8, D], BF16, kind=ik).ap()
    comb = nc.dram_tensor("comb", [D, D], BF16, kind=ik).ap()
    x2b = nc.dram_tensor("x2b", [D, D], BF16, kind="Internal").ap()
    dbgr = nc.dram_tensor("dbgr", [128, 40], F32, kind=ik if dbg else "Internal").ap()
    dbge = nc.dram_tensor("dbge", [128, 2048], F32, kind=ik if dbg else "Internal").ap()
    dbgw = nc.dram_tensor("dbgw", [128, 2048], F32, kind=ik if dbg else "Internal").ap()
    dbgx = nc.dram_tensor("dbgx", [128, 2048], F32, kind=ik if dbg else "Internal").ap()

    es = ExitStack()
    T = nc.sbuf_tensor
    PS = nc.psum_tensor
    onesb = es.enter_context(T("ones", [128, 128], BF16))
    invd = es.enter_context(T("invd", [128, 1], F32))
    epsc = es.enter_context(T("epsc", [128, 1], F32))
    nhalf = es.enter_context(T("nhalf", [128, 1], F32))
    negone = es.enter_context(T("negone", [128, 1], F32))
    idb = es.enter_context(T("idb", [128, 128], BF16))
    biasA = es.enter_context(T("biasA", [128, 16], F32))
    rb_sb = es.enter_context(T("rb_sb", [128, 8], F32))
    rw_sb = es.enter_context(T("rw_sb", [128, 128], BF16))
    eb_sb = es.enter_context(T("eb_sb", [128, 128], F32))
    ps0 = es.enter_context(PS("ps0", [128, 512], F32))
    ps1 = es.enter_context(PS("ps1", [128, 512], F32))
    psM = es.enter_context(PS("psM", [128, 512], F32))
    psV = es.enter_context(PS("psV", [128, 512], F32))
    ps2 = es.enter_context(PS("ps2", [128, 512], F32))
    psT = es.enter_context(PS("psT", [128, 128], BF16))
    psb = [ps0, ps1]

    wsl = es.enter_context(T("wsl", [128, 2, KC * 128], BF16))
    ewsl = es.enter_context(T("ewsl", [128, 2, KC * 128], BF16))
    eo = es.enter_context(T("eo", [128, 2, 512], BF16))
    otile = es.enter_context(T("otile", [128, 2, 512], BF16))

    def ln_bundle(stk, sfx):
        return dict(
            r32=stk.enter_context(T("r32" + sfx, [128, EC * 512], F32)),
            rbf=stk.enter_context(T("rbf" + sfx, [128, EC * 512], BF16)),
            rsqb=stk.enter_context(T("rsqb" + sfx, [128, EC * 512], BF16)),
            xold=stk.enter_context(T("xold" + sfx, [128, 2, 512], F32)),
            mrep=stk.enter_context(T("mrep" + sfx, [128, 512], F32)),
            vrep=stk.enter_context(T("vrep" + sfx, [128, 512], F32)),
            t32=stk.enter_context(T("t32" + sfx, [128, 512], F32)),
            rstd=stk.enter_context(T("rstd" + sfx, [128, 2, 512], F32)),
            xnf=stk.enter_context(T("xnf" + sfx, [128, 2, 512], F32)),
            xnb=stk.enter_context(T("xnb" + sfx, [128, 2, 512], BF16)),
        )

    with nc.Block() as block:
        p = P(nc)

        @block.gpsimd
        def _(g):
            p.inc(g.dma_start(out=idb[:], in_=id128[:]), "linit", 16)
            p.inc(g.dma_start(out=rb_sb[:], in_=rb[:]), "linit", 16)
            p.inc(g.dma_start(out=rw_sb[:], in_=rw[:]), "linit", 16)
            p.inc(g.dma_start(out=eb_sb[:], in_=eb[:]), "linit", 16)

        @block.vector
        def _(v):
            v.memset(onesb[:], 1.0)
            v.memset(invd[:], INV_D)
            v.memset(epsc[:], EPS)
            v.memset(nhalf[:], -0.5)
            p.inc(v.memset(negone[:], -1e9), "vinit", 1)

        def proj_ln(tag, wdram, in_sb, bias_dram, xres_dram, outf_dram, outb_dram,
                    in_ready, bnd):
            mm, ev, ca, sq, st, lnm, rs, ap, xo, ol = (tag + s for s in
                ["mm", "ev", "ca", "sq", "st", "lm", "rs", "ap", "xo", "ol"])
            r32 = bnd["r32"]; rbf = bnd["rbf"]; rsqb = bnd["rsqb"]; xold = bnd["xold"]
            mrep = bnd["mrep"]; vrep = bnd["vrep"]; t32 = bnd["t32"]; rstd = bnd["rstd"]
            xnf = bnd["xnf"]; xnb = bnd["xnb"]

            @block.gpsimd
            def _(g):
                p.inc(g.dma_start(out=biasA[:], in_=bias_dram[:]), tag + "bi", 16)

            @block.scalar
            def _(se):
                for t in range(TT):
                    for f in range(EC):
                        i = t * EC + f
                        p.wait(se, ev, i + 1)
                        if t >= 1 and f == 0:
                            p.wait(se, st, t)
                        p.inc(se.activation(out=rsqb[:, f * 512:(f + 1) * 512],
                                            in_=r32[:, f * 512:(f + 1) * 512],
                                            func=AF.Square), sq, 1)
                    p.wait(se, lnm, t + 1)
                    p.inc(se.activation(out=t32[:], in_=vrep[:], func=AF.Ln, bias=epsc[:, 0:1]),
                          tag + "sc", 1)
                    p.wait(se, tag + "sc", t + 1)
                    p.inc(se.activation(out=rstd[:, t % 2], in_=t32[:], func=AF.Exp,
                                        scale=nhalf[:, 0:1]), rs, 1)

            for t in range(TT):
                for f in range(EC):
                    i = t * EC + f

                    @block.gpsimd
                    def _(g, t=t, f=f, i=i):
                        if i >= 2:
                            p.wait(g, mm, i - 1)
                        p.inc(g.dma_start(
                            out=wsl[:, i % 2].rearrange("p (kc e) -> p kc e", e=128),
                            in_=wdram[f].rearrange("(kc p) e -> p kc e", p=128)), tag + "wl", 16)

                    @block.sync
                    def _(sy, t=t, f=f, i=i):
                        if i >= 2:
                            p.wait(sy, ev, i - 1)
                        p.inc(sy.dma_start(out=xold[:, i % 2],
                                           in_=xres_dram[f * 128:(f + 1) * 128,
                                                         t * 512:(t + 1) * 512]), xo, 16)

                    @block.tensor
                    def _(t_, t=t, f=f, i=i):
                        if i == 0:
                            p.wait(t_, "vinit", 1)
                            p.wait(t_, in_ready[0], in_ready[1])
                            p.wait(t_, tag + "bi", 16)
                        p.wait(t_, tag + "wl", 16 * (i + 1))
                        if i >= 2:
                            p.wait(t_, ev, i - 1)
                        sl = wsl[:, i % 2]
                        for kc in range(KC):
                            t_.ldweights(sl[:, kc * 128:(kc + 1) * 128])
                            mmi = t_.matmul(psb[i % 2][:], sl[:, kc * 128:(kc + 1) * 128],
                                            in_sb[:, kc * D + t * 512: kc * D + (t + 1) * 512],
                                            start=(kc == 0), stop=(kc == KC - 1))
                        p.inc(mmi, mm, 1)

                    @block.vector
                    def _(v, t=t, f=f, i=i):
                        p.wait(v, mm, i + 1)
                        p.wait(v, xo, 16 * (i + 1))
                        if f == 0 and t >= 1:
                            p.wait(v, st, t)
                        p.inc(v.scalar_tensor_tensor(r32[:, f * 512:(f + 1) * 512],
                                                     psb[i % 2][:], biasA[:, f:f + 1],
                                                     xold[:, i % 2], ALU.add, ALU.add), ev, 1)
                        p.inc(v.tensor_copy(rbf[:, f * 512:(f + 1) * 512],
                                            r32[:, f * 512:(f + 1) * 512]), ca, 1)

                    @block.scalar
                    def _(se, t=t, f=f, i=i):
                        p.wait(se, ev, i + 1)
                        if t >= 1 and f == 0:
                            p.wait(se, st, t)
                        p.inc(se.activation(out=rsqb[:, f * 512:(f + 1) * 512],
                                            in_=r32[:, f * 512:(f + 1) * 512],
                                            func=AF.Square), sq, 1)

                    @block.tensor
                    def _(t_, t=t, f=f, i=i):
                        p.wait(t_, ca, i + 1)
                        p.wait(t_, sq, i + 1)
                        if f == 0 and t >= 1:
                            p.wait(t_, lnm, t)
                        t_.ldweights(onesb[:])
                        t_.matmul(psM[:], onesb[:], rbf[:, f * 512:(f + 1) * 512],
                                  start=(f == 0), stop=(f == EC - 1))
                        mmi = t_.matmul(psV[:], onesb[:], rsqb[:, f * 512:(f + 1) * 512],
                                        start=(f == 0), stop=(f == EC - 1))
                        if f == EC - 1:
                            p.inc(mmi, st, 1)

                @block.vector
                def _(v, t=t):
                    p.wait(v, st, t + 1)
                    p.inc(v.tensor_scalar(mrep[:], psM[:], invd[:, 0:1], None, ALU.mult), "vch", 1)
                    v.tensor_scalar(vrep[:], psV[:], invd[:, 0:1], None, ALU.mult)
                    p.wait(v, "vch")
                    p.inc(v.tensor_mul(t32[:], mrep[:], mrep[:]), "vch", 1)
                    p.wait(v, "vch")
                    p.inc(v.tensor_tensor(vrep[:], vrep[:], t32[:], ALU.subtract), lnm, 1)

                for f in range(EC):
                    i = t * EC + f

                    @block.vector
                    def _(v, t=t, f=f, i=i):
                        if f == 0:
                            p.wait(v, rs, t + 1)
                        if i >= 2:
                            p.wait(v, ol, 32 * (i - 1))
                        v.tensor_tensor(xnf[:, i % 2], r32[:, f * 512:(f + 1) * 512],
                                        mrep[:], ALU.subtract)
                        v.tensor_tensor(xnf[:, i % 2], xnf[:, i % 2], rstd[:, t % 2], ALU.mult)
                        p.inc(v.tensor_copy(xnb[:, i % 2], xnf[:, i % 2]), ap, 1)

                    @block.sync
                    def _(sy, t=t, f=f, i=i):
                        p.wait(sy, ap, i + 1)
                        p.inc(sy.dma_start(out=outf_dram[f * 128:(f + 1) * 128,
                                                         t * 512:(t + 1) * 512],
                                           in_=xnf[:, i % 2]), ol, 16)
                        p.inc(sy.dma_start(out=outb_dram[f * 128:(f + 1) * 128,
                                                         t * 512:(t + 1) * 512],
                                           in_=xnb[:, i % 2]), ol, 16)
            return ol

        es_s1 = ExitStack()
        ctx_sb = es_s1.enter_context(T("ctx_sb", [128, EC * D], BF16))
        ln1 = ln_bundle(es_s1, "1")

        @block.sync
        def _(sy):
            for kc in range(KC):
                p.inc(sy.dma_start(out=ctx_sb[:, kc * D:(kc + 1) * D],
                                   in_=ctxT[kc * 128:(kc + 1) * 128, :]), "lctx", 16)

        s1_ol = proj_ln("A", wp, ctx_sb, bp, xT, x1f, x1b, ("lctx", 256), ln1)
        es_s1.close()

        es_x1 = ExitStack()
        x1_sb = es_x1.enter_context(T("x1_sb", [128, KC * D], BF16))
        es_ffn = ExitStack()
        gfm = es_ffn.enter_context(T("gfm", [8, EC * 128], BF16))
        grow = es_ffn.enter_context(T("grow", [1, 8 * D], BF16))
        grep = es_ffn.enter_context(T("grep", [128, 8 * D], BF16))
        lg = es_ffn.enter_context(T("lg", [128, 8], F32))
        m1c = es_ffn.enter_context(T("m1c", [128, 1], F32))
        nm1 = es_ffn.enter_context(T("nm1", [128, 1], F32))
        m2c = es_ffn.enter_context(T("m2c", [128, 1], F32))
        eqc = es_ffn.enter_context(T("eqc", [128, 8], F32))
        e1c = es_ffn.enter_context(T("e1c", [128, 8], F32))
        emc = es_ffn.enter_context(T("emc", [128, 1], F32))
        gts = es_ffn.enter_context(T("gts", [128, 8], F32))
        gtb = es_ffn.enter_context(T("gtb", [128, 8], BF16))
        tmp32 = es_ffn.enter_context(T("tmp32", [128, 512], F32))
        acc = es_ffn.enter_context(T("acc", [128, TT * 512], F32))
        accb = es_ffn.enter_context(T("accb", [128, TT * 512], BF16))

        @block.sync
        def _(sy):
            p.wait(sy, s1_ol, 32 * 64)
            for kc in range(KC):
                p.inc(sy.dma_start(out=x1_sb[:, kc * D:(kc + 1) * D],
                                   in_=x1b[kc * 128:(kc + 1) * 128, :]), "lx1", 16)

        @block.scalar
        def _(se):
            for tt in range(16):
                p.wait(se, "rm2", tt + 1)
                p.inc(se.activation(out=e1c[:], in_=lg[:], func=AF.Exp, bias=nm1[:, 0:1]),
                      "rex", 1)
                p.wait(se, "rex", 2 * tt + 1)
                p.inc(se.activation(out=emc[:], in_=m2c[:], func=AF.Exp, bias=nm1[:, 0:1]),
                      "rex", 1)

        for tt in range(16):
            @block.tensor
            def _(t_, tt=tt):
                if tt == 0:
                    p.wait(t_, "lx1", 256)
                    p.wait(t_, "linit", 64)
                if tt >= 1:
                    p.wait(t_, "gcp", tt)
                for kc in range(KC):
                    t_.ldweights(x1_sb[:, kc * D + tt * 128: kc * D + (tt + 1) * 128])
                    mmi = t_.matmul(ps2[:, 0:8], x1_sb[:, kc * D + tt * 128: kc * D + (tt + 1) * 128],
                                    rw_sb[:, kc * 8:(kc + 1) * 8],
                                    start=(kc == 0), stop=(kc == KC - 1))
                p.inc(mmi, "rmm", 1)

            @block.vector
            def _(v, tt=tt):
                p.wait(v, "rmm", tt + 1)
                p.inc(v.tensor_add(lg[:], ps2[:, 0:8], rb_sb[:]), "vch", 1)
                p.wait(v, "vch")
                p.inc(v.reduce_max(m1c[:], lg[:], axis=AX.X), "vch", 1)
                p.wait(v, "vch")
                p.inc(v.tensor_scalar_mul(nm1[:], m1c[:], -1.0), "vch", 1)
                p.wait(v, "vch")
                p.inc(v.tensor_scalar(eqc[:], lg[:], m1c[:, 0:1], None, ALU.is_equal), "vch", 1)
                p.wait(v, "vch")
                p.inc(v.scalar_tensor_tensor(eqc[:], eqc[:], negone[:, 0:1], lg[:],
                                             ALU.mult, ALU.add), "vch", 1)
                p.wait(v, "vch")
                p.inc(v.reduce_max(m2c[:], eqc[:], axis=AX.X), "rm2", 1)

            @block.vector
            def _(v, tt=tt):
                p.wait(v, "rex", 2 * (tt + 1))
                p.inc(v.tensor_scalar(gts[:], lg[:], m2c[:, 0:1], None, ALU.is_ge), "vch", 1)
                p.wait(v, "vch")
                p.inc(v.tensor_mul(gts[:], gts[:], e1c[:]), "vch", 1)
                p.wait(v, "vch")
                p.inc(v.tensor_scalar_add(emc[:], emc[:], 1.0), "vch", 1)
                p.wait(v, "vch")
                p.inc(v.reciprocal(emc[:], emc[:]), "vch", 1)
                p.wait(v, "vch")
                p.inc(v.tensor_scalar(gts[:], gts[:], emc[:, 0:1], None, ALU.mult), "vch", 1)
                p.wait(v, "vch")
                p.inc(v.tensor_copy(gtb[:], gts[:]), "gca", 1)

            if tt == 0:
                @block.sync
                def _(sy):
                    p.wait(sy, "gca", 1)
                    p.inc(sy.dma_start(out=dbgr[:, 0:8], in_=lg[:]), "dbgd", 16)
                    p.inc(sy.dma_start(out=dbgr[:, 8:16], in_=eqc[:]), "dbgd", 16)
                    p.inc(sy.dma_start(out=dbgr[:, 16:24], in_=e1c[:]), "dbgd", 16)
                    p.inc(sy.dma_start(out=dbgr[:, 24:32], in_=gts[:]), "dbgd", 16)
                    p.inc(sy.dma_start(out=dbgr[:, 32:33], in_=m1c[:], allow_slow_non_contiguous=True), "dbgd", 16)
                    p.inc(sy.dma_start(out=dbgr[:, 33:34], in_=m2c[:], allow_slow_non_contiguous=True), "dbgd", 16)
                    p.inc(sy.dma_start(out=dbgr[:, 34:35], in_=emc[:], allow_slow_non_contiguous=True), "dbgd", 16)

            @block.tensor
            def _(t_, tt=tt):
                p.wait(t_, "gca", tt + 1)
                if tt >= 1:
                    p.wait(t_, "gcp", tt)
                p.inc(t_.transpose(psT[0:8, :], gtb[:], idb[:]), "gtr", 1)

            @block.vector
            def _(v, tt=tt):
                p.wait(v, "gtr", tt + 1)
                p.inc(v.tensor_copy(gfm[0:8, tt * 128:(tt + 1) * 128], psT[0:8, :]), "gcp", 1)

        @block.sync
        def _(sy):
            p.wait(sy, "gcp", 16)
            p.inc(sy.dma_start(out=gdram[:], in_=gfm[0:8, :]), "gdd", 16)
            p.wait(sy, "gdd", 16)
            for e in range(8):
                p.inc(sy.dma_start(out=grow[0:1, e * D:(e + 1) * D], in_=gdram[e:e + 1, :]),
                      "grl", 16)

        for e in range(8):
            for t in range(TT):
                j = e * TT + t

                @block.tensor
                def _(t_, e=e, t=t, j=j):
                    if j == 0:
                        p.wait(t_, "grl", 128)
                    if j >= 1:
                        p.wait(t_, "gbc", j)
                    t_.ldweights(onesb[0:1, :])
                    p.inc(t_.matmul(ps2[:], onesb[0:1, :],
                                    grow[0:1, e * D + t * 512: e * D + (t + 1) * 512],
                                    start=True, stop=True), "gbm", 1)

                @block.vector
                def _(v, e=e, t=t, j=j):
                    p.wait(v, "gbm", j + 1)
                    p.inc(v.tensor_copy(grep[:, e * D + t * 512: e * D + (t + 1) * 512],
                                        ps2[:]), "gbc", 1)

        @block.scalar
        def _(se):
            for f in range(EC):
                for e in range(8):
                    n = f * 8 + e
                    for t in range(TT):
                        m = n * TT + t
                        p.wait(se, "emg", m + 1)
                        if m >= 2:
                            p.wait(se, "eev", m - 1)
                        p.inc(se.activation(out=eo[:, m % 2], in_=psb[m % 2][:],
                                            func=(AF.Identity if BISECT else AF.Gelu_apprx_tanh),
                                            bias=(0.0 if BISECT else eb_sb[:, e * 16 + f: e * 16 + f + 1])),
                              "ege", 1)

        nacc = 0
        for f in range(EC):
            for e in range(8):
                n = f * 8 + e

                if dbg and n == 1:
                    @block.gpsimd
                    def _(sy):
                        p.wait(sy, "ewl", 16)
                        p.inc(sy.dma_start(out=dbgw[:], in_=ewsl[:, 0]), "dbgwd", 16)
                        p.inc(sy.dma_start(out=dbgx[:, 512:1024], in_=x1_sb[:, 0:512]), "dbgwd", 16)

                @block.gpsimd
                def _(g, f=f, e=e, n=n):
                    if dbg and n == 2:
                        p.wait(g, "dbgwd", 32)
                    if n >= 2:
                        p.wait(g, "emg", 4 * (n - 1))
                    p.inc(g.dma_start(
                        out=ewsl[:, n % 2].rearrange("p (kc d) -> p kc d", d=128),
                        in_=ew[e, f].rearrange("(kc p) d -> p kc d", p=128)), "ewl", 16)

                @block.tensor
                def _(t_, f=f, e=e, n=n):
                    p.wait(t_, "ewl", 16 * (n + 1))
                    if n == 0:
                        p.wait(t_, "gbc", 32)
                    for t in range(TT):
                        m = n * TT + t
                        if m >= 2:
                            p.wait(t_, "eev", m - 1)
                        for kc in range(KC):
                            t_.ldweights(ewsl[:, n % 2, kc * 128:(kc + 1) * 128])
                            mmi = t_.matmul(psb[m % 2][:],
                                            ewsl[:, n % 2, kc * 128:(kc + 1) * 128],
                                            x1_sb[:, kc * D + t * 512: kc * D + (t + 1) * 512],
                                            start=(kc == 0), stop=(kc == KC - 1))
                        p.inc(mmi, "emg", 1)

                if dbg and f == 0 and e == 0:
                    @block.gpsimd
                    def _(g):
                        p.wait(g, "ege", 1)
                        p.inc(g.dma_start(out=dbgx[:, 0:512], in_=eo[:, 0]), "dbgeo", 16)

                if dbg and f == 0 and e == 1:
                    @block.sync
                    def _(sy):
                        p.wait(sy, "eev", 4)
                        p.inc(sy.dma_start(out=dbge[:], in_=acc[:]), "dbgec", 16)

                for t in range(TT):
                    m = n * TT + t

                    @block.vector
                    def _(v, f=f, e=e, n=n, t=t, m=m):
                        p.wait(v, "ege", m + 1)
                        if e == 0:
                            if dbg and f == 0 and t == 0:
                                p.wait(v, "dbgeo", 16)
                            p.inc(v.tensor_tensor(acc[:, t * 512:(t + 1) * 512], eo[:, m % 2],
                                                  grep[:, e * D + t * 512: e * D + (t + 1) * 512],
                                                  ALU.mult) if not BISECT else
                                  v.tensor_copy(acc[:, t * 512:(t + 1) * 512], eo[:, m % 2]),
                                  "eev", 1)
                        else:
                            if dbg and f == 0 and e == 1 and t == 0:
                                p.wait(v, "dbgec", 16)
                            if not BISECT:
                                v.tensor_tensor(tmp32[:], eo[:, m % 2],
                                                grep[:, e * D + t * 512: e * D + (t + 1) * 512],
                                                ALU.mult)
                            else:
                                v.tensor_copy(tmp32[:], eo[:, m % 2])
                            p.inc(v.tensor_add(acc[:, t * 512:(t + 1) * 512],
                                               acc[:, t * 512:(t + 1) * 512], tmp32[:]),
                                  "eev", 1)

            @block.vector
            def _(v, f=f):
                p.wait(v, "eev", (f * 8 + 8) * TT)
                if f >= 1:
                    p.wait(v, "cdd", 16 * f)
                p.inc(v.tensor_copy(accb[:], acc[:]), "acb", 1)

            @block.sync
            def _(sy, f=f):
                p.wait(sy, "acb", f + 1)
                p.inc(sy.dma_start(out=comb[f * 128:(f + 1) * 128, :], in_=accb[:]), "cdd", 16)

        es_ffn.close()
        es_ln2 = ExitStack()
        ln2 = ln_bundle(es_ln2, "2")
        comb_sb = x1_sb

        @block.sync
        def _(sy):
            p.wait(sy, "cdd", 256)
            for kc in range(KC):
                p.inc(sy.dma_start(out=comb_sb[:, kc * D:(kc + 1) * D],
                                   in_=comb[kc * 128:(kc + 1) * 128, :]), "lcb", 16)

        s3_ol = proj_ln("B", ow, comb_sb, ob, x1f, x2T, x2b, ("lcb", 256), ln2)

        es_ln2.close()
        es_x1.close()
        es_s4 = ExitStack()
        x2_sb = es_s4.enter_context(T("x2_sb", [128, KC * D], BF16))
        wkv_sb = es_s4.enter_context(T("wkv_sb", [128, KC * 1024], BF16))

        @block.sync
        def _(sy):
            p.wait(sy, s3_ol, 32 * 64)
            for kc in range(KC):
                p.inc(sy.dma_start(out=x2_sb[:, kc * D:(kc + 1) * D],
                                   in_=x2b[kc * 128:(kc + 1) * 128, :]), "lx2", 16)

        def proj_out(tag, wdram, nec, outdram, prev):
            for ec in range(nec):
                for t in range(TT):
                    i = ec * TT + t

                    @block.gpsimd
                    def _(g, ec=ec, t=t, i=i):
                        if t == 0:
                            if ec >= 2:
                                p.wait(g, tag + "mm", (ec - 1) * TT)
                            elif prev is not None:
                                p.wait(g, prev[0], prev[1])
                            p.inc(g.dma_start(
                                out=wsl[:, ec % 2].rearrange("p (kc e) -> p kc e", e=128),
                                in_=wdram[ec].rearrange("(kc p) e -> p kc e", p=128)),
                                tag + "wl", 16)

                    @block.tensor
                    def _(t_, ec=ec, t=t, i=i):
                        if i == 0:
                            p.wait(t_, "lx2", 256)
                            if prev is not None:
                                p.wait(t_, prev[2], prev[3])
                        if t == 0:
                            p.wait(t_, tag + "wl", 16 * (ec + 1))
                        if i >= 2:
                            p.wait(t_, tag + "ev", i - 1)
                        sl = wsl[:, ec % 2]
                        for kc in range(KC):
                            t_.ldweights(sl[:, kc * 128:(kc + 1) * 128])
                            mmi = t_.matmul(psb[i % 2][:], sl[:, kc * 128:(kc + 1) * 128],
                                            x2_sb[:, kc * D + t * 512: kc * D + (t + 1) * 512],
                                            start=(kc == 0), stop=(kc == KC - 1))
                        p.inc(mmi, tag + "mm", 1)

                    @block.vector
                    def _(v, ec=ec, t=t, i=i):
                        p.wait(v, tag + "mm", i + 1)
                        if i >= 2:
                            p.wait(v, tag + "od", 16 * (i - 1))
                        p.inc(v.tensor_copy(otile[:, i % 2], psb[i % 2][:]), tag + "ev", 1)

                    @block.sync
                    def _(sy, ec=ec, t=t, i=i):
                        p.wait(sy, tag + "ev", i + 1)
                        p.inc(sy.dma_start(out=outdram[ec * 128:(ec + 1) * 128,
                                                       t * 512:(t + 1) * 512],
                                           in_=otile[:, i % 2]), tag + "od", 16)
            return tag + "od", nec * TT * 16

        q_od, q_n = proj_out("Q", wq1, EC, q1T, ("Bmm", 64, "Bev", 64))
        k_od, k_n = proj_out("K", wkv, 8, kv1T, ("Qmm", 64, "Qev", 64))

        @block.gpsimd
        def _(g):
            for ec in range(8):
                p.inc(g.dma_start(
                    out=wkv_sb.ap().rearrange("p (kc e8) -> p kc e8", e8=1024)[:, :, ec * 128:(ec + 1) * 128],
                    in_=wkv[ec].rearrange("(kc p) e -> p kc e", p=128)), "wkvl", 16)

        for tt in range(16):
            @block.tensor
            def _(t_, tt=tt):
                if tt == 0:
                    p.wait(t_, "wkvl", 128)
                    p.wait(t_, "Kev", 32)
                for kc in range(KC):
                    t_.ldweights(x2_sb[:, kc * D + tt * 128: kc * D + (tt + 1) * 128])
                    for e2 in range(2):
                        m = tt * 2 + e2
                        if kc == 0 and m >= 2:
                            p.wait(t_, "vtev", m - 1)
                        mmi = t_.matmul(psb[e2][:],
                                        x2_sb[:, kc * D + tt * 128: kc * D + (tt + 1) * 128],
                                        wkv_sb[:, kc * 1024 + e2 * 512: kc * 1024 + (e2 + 1) * 512],
                                        start=(kc == 0), stop=(kc == KC - 1))
                p.inc(mmi, "vtmm", 1)

            @block.vector
            def _(v, tt=tt):
                p.wait(v, "vtmm", tt + 1)
                if tt == 0:
                    p.wait(v, "Kod", 512)
                if tt >= 1:
                    p.wait(v, "vtod", 16 * tt)
                for e2 in range(2):
                    p.inc(v.tensor_copy(otile[:, e2], psb[e2][:]), "vtev", 1)

            @block.sync
            def _(sy, tt=tt):
                p.wait(sy, "vtev", 2 * (tt + 1))
                p.inc(sy.dma_start(out=kvtok[tt * 128:(tt + 1) * 128, :],
                                   in_=otile.ap().rearrange("p b n -> p (b n)")), "vtod", 16)

        @block.sync
        def _(sy):
            p.wait(sy, q_od, q_n)
            p.wait(sy, k_od, k_n)
            p.wait(sy, "vtod", 256)
            p.wait(sy, s3_ol, 32 * 64)
    return nc


NH = 8
SQ = 4096
NIC = 8


def build_launch3():
    nc = bass.Bass("TRN2", target_bir_lowering=False, debug=False)
    qh_in = nc.dram_tensor("qh_in", [1024, SQ], BF16, kind="ExternalInput").ap()
    kT_in = nc.dram_tensor("kT_in", [512, SQ], BF16, kind="ExternalInput").ap()
    vt_in = nc.dram_tensor("vt_in", [SQ, 512], BF16, kind="ExternalInput").ap()
    m01 = nc.dram_tensor("m01", [128, 896], F32, kind="ExternalInput").ap()
    o1T = nc.dram_tensor("o1T", [1024, SQ], BF16, kind="ExternalOutput").ap()

    es = ExitStack()
    T = nc.sbuf_tensor
    PS = nc.psum_tensor
    m01b = es.enter_context(T("m01b", [128, 896], BF16))
    onesb = es.enter_context(T("ones", [128, 128], BF16))
    scol = es.enter_context(T("scol", [128, 1], F32))
    bcol = es.enter_context(T("bcol", [128, 1], F32))
    qh = es.enter_context(T("qh", [128, 2, SQ], BF16))
    kh = es.enter_context(T("kh", [128, 2, SQ], BF16))
    vct = es.enter_context(T("vct", [128, 2, SQ], BF16))
    ptb = es.enter_context(T("ptb", [128, 2, 512], BF16))
    lr = es.enter_context(T("lr", [128, 512], F32))
    ctxa = es.enter_context(T("ctxa", [128, 2, SQ], BF16))
    psS0 = es.enter_context(PS("psS0", [128, 512], F32))
    psS1 = es.enter_context(PS("psS1", [128, 512], F32))
    psO = es.enter_context(PS("psO", [128, 512], F32))
    psL = es.enter_context(PS("psL", [128, 512], F32))
    psS = [psS0, psS1]

    with nc.Block() as block:
        p = P(nc)

        @block.gpsimd
        def _(g):
            p.inc(g.dma_start(out=m01b[:], in_=m01[:]), "lx", 16)

        @block.vector
        def _(v):
            v.memset(onesb[:], 1.0)
            v.memset(scol[:], SCALE)
            p.inc(v.memset(bcol[:], BSHIFT), "vinit", 1)

        @block.scalar
        def _(se):
            for h in range(NH):
                for ic in range(NIC):
                    nt = ntiles_of(ic)
                    base = tiles_before(h, ic)
                    if base > 0:
                        p.wait(se, "pexp", base)
                    for j in range(nt):
                        gt = base + j + 1
                        p.wait(se, "smm", gt)
                        if gt - 2 >= 1:
                            p.wait(se, "pvl", gt - 2)
                        diag = (j >= nt - 4)
                        p.inc(se.activation(out=ptb[:, j % 2], in_=psS[j % 2][:],
                                            func=AF.Exp, bias=bcol[:, 0:1],
                                            scale=scol[:, 0:1]), "eraw" if diag else "pexp", 1)

        for h in range(NH):
            hb = h % 2
            g_ = h // 2

            @block.sync
            def _(sy, h=h, hb=hb, g_=g_):
                if h >= 2:
                    p.wait(sy, "pvl", tiles_before(h - 1, 0))
                p.inc(sy.dma_start(out=qh[:, hb], in_=qh_in[h * 128:(h + 1) * 128, :]), "hload", 16)
                p.inc(sy.dma_start(out=kh[:, hb], in_=kT_in[g_ * 128:(g_ + 1) * 128, :]), "hload", 16)
                for jt in range(32):
                    p.inc(sy.dma_start(out=vct[:, hb, jt * 128:(jt + 1) * 128],
                                       in_=vt_in[jt * 128:(jt + 1) * 128, g_ * 128:(g_ + 1) * 128]),
                          "hload", 16)

            for ic in range(NIC):
                nt = ntiles_of(ic)
                base = tiles_before(h, ic)

                @block.tensor
                def _(t_, h=h, hb=hb, ic=ic, nt=nt, base=base):
                    if ic == 0:
                        p.wait(t_, "hload", 16 * 34 * (h + 1))
                        if h == 0:
                            p.wait(t_, "vinit", 1)
                            p.wait(t_, "lx", 16)
                    if base > 0:
                        p.wait(t_, "norm", h * NIC + ic)
                        p.wait(t_, "pexp", base)

                    def mm_s(t_, j):
                        t_.ldweights(kh[:, hb, j * 128:(j + 1) * 128])
                        p.inc(t_.matmul(psS[j % 2][:], kh[:, hb, j * 128:(j + 1) * 128],
                                        qh[:, hb, ic * 512:(ic + 1) * 512],
                                        start=True, stop=True), "smm", 1)

                    mm_s(t_, 0)
                    mm_s(t_, 1)
                    for j in range(nt):
                        gt = base + j + 1
                        p.wait(t_, "pexp", gt)
                        t_.ldweights(vct[:, hb, j * 128:(j + 1) * 128])
                        t_.matmul(psO[:], vct[:, hb, j * 128:(j + 1) * 128], ptb[:, j % 2],
                                  start=(j == 0), stop=(j == nt - 1))
                        t_.ldweights(onesb[:])
                        p.inc(t_.matmul(psL[:], onesb[:], ptb[:, j % 2],
                                        start=(j == 0), stop=(j == nt - 1)), "pvl", 1)
                        if j + 2 < nt:
                            mm_s(t_, j + 2)

                @block.vector
                def _(v, h=h, ic=ic, nt=nt, base=base):
                    for r, j in enumerate(range(nt - 4, nt)):
                        p.wait(v, "eraw", (h * NIC + ic) * 4 + r + 1)
                        off = 384 - 128 * (j - (nt - 4))
                        p.inc(v.tensor_mul(ptb[:, j % 2], ptb[:, j % 2],
                                           m01b[:, off:off + 512]), "pexp", 1)
                    p.wait(v, "pvl", base + nt)
                    if h >= 2 and ic == 0:
                        p.wait(v, "cdump", 16 * (h - 1))
                    v.reciprocal(lr[:], psL[:])
                    p.inc(v.tensor_mul(ctxa[:, h % 2, ic * 512:(ic + 1) * 512], psO[:], lr[:]),
                          "norm", 1)

            @block.sync
            def _(sy, h=h):
                p.wait(sy, "norm", NIC * (h + 1))
                p.inc(sy.dma_start(out=o1T[h * 128:(h + 1) * 128, :], in_=ctxa[:, h % 2]),
                      "cdump", 16)

        @block.sync
        def _(sy):
            p.wait(sy, "cdump", 16 * NH)
    return nc


EC = 16
TT = 4
INV_D = 1.0 / 2048.0
EPS = 1e-5


def build_launch4(skip_counts=False):
    nc = bass.Bass("TRN2", target_bir_lowering=False, debug=False)
    ctxT = nc.dram_tensor("ctxT", [D, D], BF16, kind="ExternalInput").ap()
    x2T = nc.dram_tensor("x2T", [D, D], F32, kind="ExternalInput").ap()
    wd = nc.dram_tensor("wd", [EC, D, 128], F32, kind="ExternalInput").ap()
    zb = nc.dram_tensor("zb", [128, 16], F32, kind="ExternalInput").ap()
    rw1 = nc.dram_tensor("rw1", [128, 128], F32, kind="ExternalInput").ap()
    x3T = nc.dram_tensor("x3T", [D, D], F32, kind="ExternalOutput").ap()
    x3bT = nc.dram_tensor("x3bT", [D, D], BF16, kind="ExternalOutput").ap()
    cnts = nc.dram_tensor("cnts", [1, 16], F32, kind="ExternalOutput").ap()

    es = ExitStack()
    T = nc.sbuf_tensor
    PS = nc.psum_tensor
    onesb = es.enter_context(T("ones", [128, 128], BF16))
    invd = es.enter_context(T("invd", [128, 1], F32))
    epsc = es.enter_context(T("epsc", [128, 1], F32))
    nhalf = es.enter_context(T("nhalf", [128, 1], F32))
    negbig = es.enter_context(T("negbig", [128, 1], F32))
    biasA = es.enter_context(T("biasA", [128, 16], F32))
    rw_sb = es.enter_context(T("rw_sb", [128, 128], BF16))
    wsl = es.enter_context(T("wsl", [128, 2, KC * 128], BF16))
    lg = es.enter_context(T("lg", [128, 8], F32))
    m1c = es.enter_context(T("m1c", [128, 1], F32))
    m2c = es.enter_context(T("m2c", [128, 1], F32))
    eqc = es.enter_context(T("eqc", [128, 8], F32))
    eq1b = es.enter_context(T("eq1b", [128, 8], BF16))
    eq2b = es.enter_context(T("eq2b", [128, 8], BF16))
    csb = es.enter_context(T("csb", [1, 16], F32))
    ps0 = es.enter_context(PS("ps0", [128, 512], F32))
    ps1 = es.enter_context(PS("ps1", [128, 512], F32))
    psM = es.enter_context(PS("psM", [128, 512], F32))
    psV = es.enter_context(PS("psV", [128, 512], F32))
    psG = es.enter_context(PS("psG", [128, 8], F32))
    psC1 = es.enter_context(PS("psC1", [1, 8], F32))
    psC2 = es.enter_context(PS("psC2", [1, 8], F32))
    psb = [ps0, ps1]

    with nc.Block() as block:
        p = P(nc)

        @block.gpsimd
        def _(g):
            p.inc(g.dma_start(out=rw_sb[:], in_=rw1[:]), "linit", 16)
            p.inc(g.dma_start(out=biasA[:], in_=zb[:]), "linit", 16)

        @block.vector
        def _(v):
            v.memset(csb[:], 0.0)
            v.memset(onesb[:], 1.0)
            v.memset(invd[:], INV_D)
            v.memset(epsc[:], EPS)
            v.memset(nhalf[:], -0.5)
            p.inc(v.memset(negbig[:], -1e9), "vinit", 1)

        es_s1 = ExitStack()
        ctx_sb = es_s1.enter_context(T("ctx_sb", [128, EC * D], BF16))
        r32 = es_s1.enter_context(T("r32", [128, EC * 512], F32))
        rbf = es_s1.enter_context(T("rbf", [128, EC * 512], BF16))
        rsqb = es_s1.enter_context(T("rsqb", [128, EC * 512], BF16))
        xold = es_s1.enter_context(T("xold", [128, 2, 512], F32))
        mrep = es_s1.enter_context(T("mrep", [128, 512], F32))
        vrep = es_s1.enter_context(T("vrep", [128, 512], F32))
        t32 = es_s1.enter_context(T("t32", [128, 512], F32))
        rstd = es_s1.enter_context(T("rstd", [128, 2, 512], F32))
        xnf = es_s1.enter_context(T("xnf", [128, 2, 512], F32))
        xnb = es_s1.enter_context(T("xnb", [128, 2, 512], BF16))

        @block.sync
        def _(sy):
            for kc in range(KC):
                p.inc(sy.dma_start(out=ctx_sb[:, kc * D:(kc + 1) * D],
                                   in_=ctxT[kc * 128:(kc + 1) * 128, :]), "lctx", 16)

        @block.scalar
        def _(se):
            for t in range(TT):
                for f in range(EC):
                    i = t * EC + f
                    p.wait(se, "Aev", i + 1)
                    if t >= 1 and f == 0:
                        p.wait(se, "Ast", t)
                    p.inc(se.activation(out=rsqb[:, f * 512:(f + 1) * 512],
                                        in_=r32[:, f * 512:(f + 1) * 512],
                                        func=AF.Square), "Asq", 1)
                p.wait(se, "Alm", t + 1)
                p.inc(se.activation(out=t32[:], in_=vrep[:], func=AF.Ln, bias=epsc[:, 0:1]),
                      "sch", 1)
                p.wait(se, "sch", t + 1)
                p.inc(se.activation(out=rstd[:, t % 2], in_=t32[:], func=AF.Exp,
                                    scale=nhalf[:, 0:1]), "Ars", 1)

        for t in range(TT):
            for f in range(EC):
                i = t * EC + f

                @block.gpsimd
                def _(g, t=t, f=f, i=i):
                    if i >= 2:
                        p.wait(g, "Amm", i - 1)
                    p.inc(g.dma_start(
                        out=wsl[:, i % 2].rearrange("p (kc e) -> p kc e", e=128),
                        in_=wd[f].rearrange("(kc p) e -> p kc e", p=128)), "Awl", 16)

                @block.sync
                def _(sy, t=t, f=f, i=i):
                    if i >= 2:
                        p.wait(sy, "Aev", i - 1)
                    p.inc(sy.dma_start(out=xold[:, i % 2],
                                       in_=x2T[f * 128:(f + 1) * 128,
                                               t * 512:(t + 1) * 512]), "Axo", 16)

                @block.tensor
                def _(t_, t=t, f=f, i=i):
                    if i == 0:
                        p.wait(t_, "vinit", 1)
                        p.wait(t_, "lctx", 256)
                        p.wait(t_, "linit", 32)
                    p.wait(t_, "Awl", 16 * (i + 1))
                    if i >= 2:
                        p.wait(t_, "Aev", i - 1)
                    sl = wsl[:, i % 2]
                    for kc in range(KC):
                        t_.ldweights(sl[:, kc * 128:(kc + 1) * 128])
                        mmi = t_.matmul(psb[i % 2][:], sl[:, kc * 128:(kc + 1) * 128],
                                        ctx_sb[:, kc * D + t * 512: kc * D + (t + 1) * 512],
                                        start=(kc == 0), stop=(kc == KC - 1))
                    p.inc(mmi, "Amm", 1)

                @block.vector
                def _(v, t=t, f=f, i=i):
                    p.wait(v, "Amm", i + 1)
                    p.wait(v, "Axo", 16 * (i + 1))
                    if f == 0 and t >= 1:
                        p.wait(v, "Ast", t)
                    p.inc(v.scalar_tensor_tensor(r32[:, f * 512:(f + 1) * 512],
                                                 psb[i % 2][:], biasA[:, f:f + 1],
                                                 xold[:, i % 2], ALU.add, ALU.add), "Aev", 1)
                    p.inc(v.tensor_copy(rbf[:, f * 512:(f + 1) * 512],
                                        r32[:, f * 512:(f + 1) * 512]), "Aca", 1)

                @block.tensor
                def _(t_, t=t, f=f, i=i):
                    p.wait(t_, "Aca", i + 1)
                    p.wait(t_, "Asq", i + 1)
                    if f == 0 and t >= 1:
                        p.wait(t_, "Alm", t)
                    t_.ldweights(onesb[:])
                    t_.matmul(psM[:], onesb[:], rbf[:, f * 512:(f + 1) * 512],
                              start=(f == 0), stop=(f == EC - 1))
                    mmi = t_.matmul(psV[:], onesb[:], rsqb[:, f * 512:(f + 1) * 512],
                                    start=(f == 0), stop=(f == EC - 1))
                    if f == EC - 1:
                        p.inc(mmi, "Ast", 1)

            @block.vector
            def _(v, t=t):
                p.wait(v, "Ast", t + 1)
                p.inc(v.tensor_scalar(mrep[:], psM[:], invd[:, 0:1], None, ALU.mult), "vch", 1)
                v.tensor_scalar(vrep[:], psV[:], invd[:, 0:1], None, ALU.mult)
                p.wait(v, "vch")
                p.inc(v.tensor_mul(t32[:], mrep[:], mrep[:]), "vch", 1)
                p.wait(v, "vch")
                p.inc(v.tensor_tensor(vrep[:], vrep[:], t32[:], ALU.subtract), "Alm", 1)

            for f in range(EC):
                i = t * EC + f

                @block.vector
                def _(v, t=t, f=f, i=i):
                    if f == 0:
                        p.wait(v, "Ars", t + 1)
                    if i >= 2:
                        p.wait(v, "Aol", 32 * (i - 1))
                    v.tensor_tensor(xnf[:, i % 2], r32[:, f * 512:(f + 1) * 512],
                                    mrep[:], ALU.subtract)
                    v.tensor_tensor(xnf[:, i % 2], xnf[:, i % 2], rstd[:, t % 2], ALU.mult)
                    p.inc(v.tensor_copy(xnb[:, i % 2], xnf[:, i % 2]), "Aap", 1)

                @block.sync
                def _(sy, t=t, f=f, i=i):
                    p.wait(sy, "Aap", i + 1)
                    p.inc(sy.dma_start(out=x3T[f * 128:(f + 1) * 128, t * 512:(t + 1) * 512],
                                       in_=xnf[:, i % 2]), "Aol", 16)
                    p.inc(sy.dma_start(out=x3bT[f * 128:(f + 1) * 128, t * 512:(t + 1) * 512],
                                       in_=xnb[:, i % 2]), "Aol", 16)

        es_s1.close()
        if skip_counts:
            @block.sync
            def _(sy):
                p.wait(sy, "Aol", 32 * 64)

            @block.vector
            def _(v):
                p.inc(v.memset(csb[:], 0.0), "Ccp0", 1)

            @block.sync
            def _(sy):
                p.wait(sy, "Ccp0", 1)
                p.inc(sy.dma_start(out=cnts[:], in_=csb[0:1, :]), "Cod", 16)
                sy.wait_ge(p.sems["Cod"], 16)
            return nc
        x3_sb = es.enter_context(T("x3_sb", [128, KC * D], BF16))

        @block.sync
        def _(sy):
            p.wait(sy, "Aol", 32 * 64)
            for kc in range(KC):
                p.inc(sy.dma_start(out=x3_sb[:, kc * D:(kc + 1) * D],
                                   in_=x3bT[kc * 128:(kc + 1) * 128, :]), "lx3", 16)

        for tt in range(16):
            @block.tensor
            def _(t_, tt=tt):
                if tt == 0:
                    p.wait(t_, "lx3", 256)
                if tt >= 1:
                    p.wait(t_, "Rdone", tt)
                for kc in range(KC):
                    t_.ldweights(x3_sb[:, kc * D + tt * 128: kc * D + (tt + 1) * 128])
                    mmi = t_.matmul(psG[:], x3_sb[:, kc * D + tt * 128: kc * D + (tt + 1) * 128],
                                    rw_sb[:, kc * 8:(kc + 1) * 8],
                                    start=(kc == 0), stop=(kc == KC - 1))
                p.inc(mmi, "Rmm", 1)

            @block.vector
            def _(v, tt=tt):
                p.wait(v, "Rmm", tt + 1)
                p.inc(v.tensor_copy(lg[:], psG[:]), "vch", 1)
                p.wait(v, "vch")
                p.inc(v.reduce_max(m1c[:], lg[:], axis=AX.X), "vch", 1)
                p.wait(v, "vch")
                p.inc(v.tensor_scalar(eqc[:], lg[:], m1c[:, 0:1], None, ALU.is_equal), "vch", 1)
                p.wait(v, "vch")
                p.inc(v.tensor_copy(eq1b[:], eqc[:]), "vch", 1)
                p.wait(v, "vch")
                p.inc(v.scalar_tensor_tensor(eqc[:], eqc[:], negbig[:, 0:1], lg[:],
                                             ALU.mult, ALU.add), "vch", 1)
                p.wait(v, "vch")
                p.inc(v.reduce_max(m2c[:], eqc[:], axis=AX.X), "vch", 1)
                p.wait(v, "vch")
                p.inc(v.tensor_scalar(eqc[:], lg[:], m2c[:, 0:1], None, ALU.is_equal), "vch", 1)
                p.wait(v, "vch")
                p.inc(v.tensor_copy(eq2b[:], eqc[:]), "Req", 1)

            @block.tensor
            def _(t_, tt=tt):
                p.wait(t_, "Req", tt + 1)
                if tt >= 1:
                    p.wait(t_, "Cacc", 2 * tt)
                t_.ldweights(onesb[:, 0:1])
                t_.matmul(psC1[:], onesb[:, 0:1], eq1b[:], start=True, stop=True)
                t_.ldweights(onesb[:, 0:1])
                mmi = t_.matmul(psC2[:], onesb[:, 0:1], eq2b[:], start=True, stop=True)
                p.inc(mmi, "Rdone", 1)

            @block.vector
            def _(v, tt=tt):
                p.wait(v, "Rdone", tt + 1)
                p.inc(v.tensor_add(csb[0:1, 0:8], csb[0:1, 0:8], psC1[:]), "Cacc", 1)
                p.wait(v, "Cacc")
                p.inc(v.tensor_add(csb[0:1, 8:16], csb[0:1, 8:16], psC2[:]), "Cacc", 1)

        @block.vector
        def _(v):
            p.wait(v, "Cacc", 32)
            p.inc(v.nop(), "Ccp", 2)

        @block.sync
        def _(sy):
            p.wait(sy, "Ccp", 2)
            p.inc(sy.dma_start(out=cnts[:], in_=csb[0:1, :]), "Cod", 16)
            p.wait(sy, "Cod", 16)
            p.wait(sy, "Aol", 32 * 64)
    return nc


EC = 16
TT = 4
FC = 64
INV_D = 1.0 / 2048.0
EPS = 1e-5
CAP = 512.0


def build_launch5():
    nc = bass.Bass("TRN2", target_bir_lowering=False, debug=False)
    x3T = nc.dram_tensor("x3T", [D, D], F32, kind="ExternalInput").ap()
    x3bT = nc.dram_tensor("x3bT", [D, D], BF16, kind="ExternalInput").ap()
    rw1 = nc.dram_tensor("rw1", [128, 128], F32, kind="ExternalInput").ap()
    G = nc.dram_tensor("G", [16, 8], F32, kind="ExternalInput").ap()
    wsel = nc.dram_tensor("wsel", [16, 4], F32, kind="ExternalInput").ap()
    sltI = nc.dram_tensor("sltI", [128, 128], F32, kind="ExternalInput").ap()
    w1c = nc.dram_tensor("w1c", [FC, D, 128], F32, kind="ExternalInput").ap()
    b1c = nc.dram_tensor("b1c", [128, FC], F32, kind="ExternalInput").ap()
    w2c = nc.dram_tensor("w2c", [EC, 8192, 128], F32, kind="ExternalInput").ap()
    b2c = nc.dram_tensor("b2c", [128, 16], F32, kind="ExternalInput").ap()
    outT = nc.dram_tensor("outT", [D, D], F32, kind="ExternalOutput").ap()
    wdram = nc.dram_tensor("wdram", [1, D], F32, kind="Internal").ap()
    hdram = nc.dram_tensor("hdram", [D, D], BF16, kind="Internal").ap()

    es = ExitStack()
    T = nc.sbuf_tensor
    PS = nc.psum_tensor
    onesb = es.enter_context(T("ones", [128, 128], BF16))
    ones1f = es.enter_context(T("ones1f", [1, 128], F32))
    invd = es.enter_context(T("invd", [128, 1], F32))
    epsc = es.enter_context(T("epsc", [128, 1], F32))
    nhalf = es.enter_context(T("nhalf", [128, 1], F32))
    negbig = es.enter_context(T("negbig", [128, 1], F32))
    rw_sb = es.enter_context(T("rw_sb", [128, 128], BF16))
    sltb = es.enter_context(T("sltb", [128, 128], BF16))
    G_sb = es.enter_context(T("G_sb", [16, 8], F32))
    wsel_sb = es.enter_context(T("wsel_sb", [16, 4], F32))
    b1_sb = es.enter_context(T("b1_sb", [128, FC], F32))
    b2_sb = es.enter_context(T("b2_sb", [128, 16], F32))
    offr = es.enter_context(T("offr", [1, 24], F32))
    lg = es.enter_context(T("lg", [128, 8], F32))
    m1c = es.enter_context(T("m1c", [128, 1], F32))
    nm1 = es.enter_context(T("nm1", [128, 1], F32))
    m2c = es.enter_context(T("m2c", [128, 1], F32))
    eqc = es.enter_context(T("eqc", [128, 8], F32))
    eq1 = es.enter_context(T("eq1", [128, 8], F32))
    eq2 = es.enter_context(T("eq2", [128, 8], F32))
    eq1b = es.enter_context(T("eq1b", [128, 8], BF16))
    eq2b = es.enter_context(T("eq2b", [128, 8], BF16))
    exs = es.enter_context(T("exs", [128, 8], F32))
    zc = es.enter_context(T("zc", [128, 1], F32))
    emc = es.enter_context(T("emc", [128, 1], F32))
    ind = es.enter_context(T("ind", [128, 8], F32))
    k1c = es.enter_context(T("k1c", [128, 1], F32))
    k2c = es.enter_context(T("k2c", [128, 1], F32))
    wcol = es.enter_context(T("wcol", [128, 1], F32))
    wrow = es.enter_context(T("wrow", [1, D], F32))
    wrep = es.enter_context(T("wrep", [128, TT * 512], F32))
    x3_sb = es.enter_context(T("x3_sb", [128, KC * D], BF16))
    htile = es.enter_context(T("htile", [128, 2, 512], BF16))
    ps0 = es.enter_context(PS("ps0", [128, 512], F32))
    ps1 = es.enter_context(PS("ps1", [128, 512], F32))
    psM = es.enter_context(PS("psM", [128, 512], F32))
    psV = es.enter_context(PS("psV", [128, 512], F32))
    psG = es.enter_context(PS("psG", [128, 8], F32))
    psP = es.enter_context(PS("psP", [128, 8], F32))
    psW = es.enter_context(PS("psW", [128, 512], F32))
    psb = [ps0, ps1]

    with nc.Block() as block:
        p = P(nc)

        @block.gpsimd
        def _(g):
            p.inc(g.dma_start(out=rw_sb[:], in_=rw1[:]), "linit", 16)
            p.inc(g.dma_start(out=sltb[:], in_=sltI[:]), "linit", 16)
            p.inc(g.dma_start(out=G_sb[:], in_=G[:]), "linit", 16)
            p.inc(g.dma_start(out=wsel_sb[:], in_=wsel[:]), "linit", 16)
            p.inc(g.dma_start(out=b1_sb[:], in_=b1c[:]), "linit", 16)
            p.inc(g.dma_start(out=b2_sb[:], in_=b2c[:]), "linit", 16)

        @block.vector
        def _(v):
            v.memset(onesb[:], 1.0)
            v.memset(ones1f[:], 1.0)
            v.memset(invd[:], INV_D)
            v.memset(epsc[:], EPS)
            v.memset(nhalf[:], -0.5)
            p.inc(v.memset(negbig[:], -1e9), "vinit", 1)

        @block.sync
        def _(sy):
            for kc in range(KC):
                p.inc(sy.dma_start(out=x3_sb[:, kc * D:(kc + 1) * D],
                                   in_=x3bT[kc * 128:(kc + 1) * 128, :]), "lx3", 16)

        @block.tensor
        def _(t_):
            p.wait(t_, "linit", 96)
            p.wait(t_, "vinit", 1)
            for s in range(3):
                mmi = t_.matmul(psP[0:1, 0:8], wsel_sb[0:16, s:s + 1], G_sb[0:16, :],
                                start=True, stop=True)
                p.inc(mmi, "Omm", 1)
                p.wait(t_, "Oev", s + 1) if s < 2 else None

        for s in range(3):
            @block.vector
            def _(v, s=s):
                p.wait(v, "Omm", s + 1)
                p.inc(v.tensor_copy(offr[0:1, 8 * s:8 * s + 8], psP[0:1, 0:8]), "Oev", 1)

        @block.vector
        def _(v):
            p.wait(v, "Oev", 3)
            p.inc(v.tensor_add(offr[0:1, 16:24], offr[0:1, 16:24], offr[0:1, 8:16]),
                  "Ooff", 1)

        run1 = es.enter_context(T("run1", [1, 8], F32))
        run2 = es.enter_context(T("run2", [1, 8], F32))

        @block.vector
        def _(v):
            v.memset(run1[:], 0.0)
            p.inc(v.memset(run2[:], 0.0), "Orun", 1)

        @block.scalar
        def _(se):
            for tt in range(16):
                p.wait(se, "Req", tt + 1)
                p.inc(se.activation(out=exs[:], in_=lg[:], func=AF.Exp, bias=nm1[:, 0:1]),
                      "sch", 1)
                p.wait(se, "sch", tt + 1)
                p.inc(se.activation(out=emc[:], in_=m2c[:], func=AF.Exp, bias=nm1[:, 0:1]),
                      "Rex", 1)

        for tt in range(16):
            @block.tensor
            def _(t_, tt=tt):
                if tt == 0:
                    p.wait(t_, "lx3", 256)
                if tt >= 1:
                    p.wait(t_, "Wev", tt)
                for kc in range(KC):
                    t_.ldweights(x3_sb[:, kc * D + tt * 128: kc * D + (tt + 1) * 128])
                    mmi = t_.matmul(psG[:], x3_sb[:, kc * D + tt * 128: kc * D + (tt + 1) * 128],
                                    rw_sb[:, kc * 8:(kc + 1) * 8],
                                    start=(kc == 0), stop=(kc == KC - 1))
                p.inc(mmi, "Rmm", 1)

            @block.vector
            def _(v, tt=tt):
                p.wait(v, "Rmm", tt + 1)
                if tt >= 1:
                    p.wait(v, "Rex", tt)
                    p.wait(v, "Wcol", tt)
                p.inc(v.tensor_copy(lg[:], psG[:]), "Wev", 1)
                p.wait(v, "Wev")
                p.inc(v.reduce_max(m1c[:], lg[:], axis=AX.X), "vch", 1)
                p.wait(v, "vch")
                p.inc(v.tensor_scalar_mul(nm1[:], m1c[:], -1.0), "vch", 1)
                p.wait(v, "vch")
                p.inc(v.tensor_scalar(eq1[:], lg[:], m1c[:, 0:1], None, ALU.is_equal), "vch", 1)
                p.wait(v, "vch")
                p.inc(v.scalar_tensor_tensor(eqc[:], eq1[:], negbig[:, 0:1], lg[:],
                                             ALU.mult, ALU.add), "vch", 1)
                p.wait(v, "vch")
                p.inc(v.reduce_max(m2c[:], eqc[:], axis=AX.X), "vch", 1)
                p.wait(v, "vch")
                p.inc(v.tensor_scalar(eq2[:], lg[:], m2c[:, 0:1], None, ALU.is_equal), "vch", 1)
                p.wait(v, "vch")
                p.inc(v.tensor_copy(eq1b[:], eq1[:]), "vch", 1)
                p.wait(v, "vch")
                p.inc(v.tensor_copy(eq2b[:], eq2[:]), "Req", 1)

            for which in (1, 2):
                idx = 2 * tt + which

                @block.vector
                def _(v, tt=tt, which=which, idx=idx):
                    runX = run1 if which == 1 else run2
                    offsl = offr[0:1, 0:8] if which == 1 else offr[0:1, 16:24]
                    if tt == 0 and which == 1:
                        p.wait(v, "Ooff", 1)
                        p.wait(v, "Orun", 1)
                    if tt >= 1:
                        p.wait(v, "Prun2", 2 * tt)
                    p.wait(v, "Req", tt + 1)
                    p.inc(v.tensor_add(runX[0:1, :], runX[0:1, :], offsl), "Prow", 1)

                @block.tensor
                def _(t_, tt=tt, which=which, idx=idx):
                    eqb = eq1b if which == 1 else eq2b
                    runX = run1 if which == 1 else run2
                    p.wait(t_, "Prow", idx)
                    if idx >= 2:
                        p.wait(t_, "Prun2", idx - 1)
                    t_.ldweights(sltb[:])
                    t_.matmul(psP[:], sltb[:], eqb[:], start=True, stop=False)
                    mmi = t_.matmul(psP[:], ones1f[0:1, :], runX[0:1, :],
                                    start=False, stop=True)
                    p.inc(mmi, "Pmm", 1)

                @block.vector
                def _(v, tt=tt, which=which, idx=idx):
                    runX = run1 if which == 1 else run2
                    offsl = offr[0:1, 0:8] if which == 1 else offr[0:1, 16:24]
                    eqf = eq1 if which == 1 else eq2
                    kc_ = k1c if which == 1 else k2c
                    p.wait(v, "Pmm", idx)
                    p.inc(v.tensor_scalar(ind[:], psP[:], CAP, None, ALU.is_lt), "vch", 1)
                    p.wait(v, "vch")
                    p.inc(v.tensor_mul(ind[:], ind[:], eqf[:]), "vch", 1)
                    p.wait(v, "vch")
                    p.inc(v.reduce_sum(kc_[:], ind[:], axis=AX.X), "Pkeep", 1)
                    p.inc(v.tensor_tensor(runX[0:1, :], runX[0:1, :], offsl, ALU.subtract),
                          "Pres", 1)

                @block.tensor
                def _(t_, tt=tt, which=which, idx=idx):
                    eqb = eq1b if which == 1 else eq2b
                    p.wait(t_, "Pkeep", idx)
                    t_.ldweights(onesb[:, 0:1])
                    p.inc(t_.matmul(psP[0:1, 0:8], onesb[:, 0:1], eqb[:],
                                    start=True, stop=True), "Csum", 1)

                @block.vector
                def _(v, tt=tt, which=which, idx=idx):
                    runX = run1 if which == 1 else run2
                    p.wait(v, "Csum", idx)
                    p.wait(v, "Pres", idx)
                    p.inc(v.tensor_add(runX[0:1, :], runX[0:1, :], psP[0:1, 0:8]), "Prun2", 1)

            @block.vector
            def _(v, tt=tt):
                p.wait(v, "Rex", tt + 1)
                p.wait(v, "Pkeep", 2 * tt + 2)
                p.inc(v.reduce_sum(zc[:], exs[:], axis=AX.X), "vch", 1)
                p.wait(v, "vch")
                p.inc(v.reciprocal(zc[:], zc[:]), "vch", 1)
                p.wait(v, "vch")
                p.inc(v.tensor_mul(wcol[:], emc[:], k2c[:]), "vch", 1)
                p.wait(v, "vch")
                p.inc(v.tensor_add(wcol[:], wcol[:], k1c[:]), "vch", 1)
                p.wait(v, "vch")
                p.inc(v.tensor_mul(wcol[:], wcol[:], zc[:]), "Wcol", 1)

            @block.sync
            def _(sy, tt=tt):
                p.wait(sy, "Wcol", tt + 1)
                p.inc(sy.dma_start(out=wdram[0:1, tt * 128:(tt + 1) * 128],
                                   in_=wcol[:], allow_slow_non_contiguous=True), "Wod", 16)

        @block.sync
        def _(sy):
            p.wait(sy, "Wod", 256)
            p.inc(sy.dma_start(out=wrow[:], in_=wdram[:]), "Wrl", 16)

        wrowb = es.enter_context(T("wrowb", [1, D], BF16))

        @block.vector
        def _(v):
            p.wait(v, "Wrl", 16)
            p.inc(v.tensor_copy(wrowb[:], wrow[:]), "Wrb", 1)

        for t in range(TT):
            @block.tensor
            def _(t_, t=t):
                p.wait(t_, "Wrb", 1)
                if t >= 1:
                    p.wait(t_, "Wrc", t)
                t_.ldweights(onesb[0:1, :])
                p.inc(t_.matmul(psW[:], onesb[0:1, :], wrowb[0:1, t * 512:(t + 1) * 512],
                                start=True, stop=True), "Wbm", 1)

            @block.vector
            def _(v, t=t):
                p.wait(v, "Wbm", t + 1)
                p.inc(v.tensor_copy(wrep[:, t * 512:(t + 1) * 512], psW[:]), "Wrc", 1)

        es_ffn = ExitStack()
        wsl = es_ffn.enter_context(T("wsl", [128, 2, KC * 128], BF16))
        w2sl = es_ffn.enter_context(T("w2sl", [128, 2, FC * 128], BF16))
        A_sb = es_ffn.enter_context(T("A_sb", [128, FC * 512], BF16))

        @block.scalar
        def _(se):
            for t in range(TT):
                for fc in range(FC):
                    i = t * FC + fc
                    p.wait(se, "Fmm", i + 1)
                    p.inc(se.activation(out=A_sb[:, fc * 512:(fc + 1) * 512],
                                        in_=psb[i % 2][:], func=AF.Gelu,
                                        bias=b1_sb[:, fc:fc + 1]), "Fge", 1)
                for fo in range(EC):
                    j = t * EC + fo
                    p.wait(se, "Hmm", j + 1)
                    if j >= 2:
                        p.wait(se, "Hod", 16 * (j - 1))
                    p.inc(se.activation(out=htile[:, j % 2], in_=psb[j % 2][:],
                                        func=AF.Identity, bias=b2_sb[:, fo:fo + 1]), "Hev", 1)

        for t in range(TT):
            for fc in range(FC):
                i = t * FC + fc

                @block.gpsimd
                def _(g, t=t, fc=fc, i=i):
                    if i >= 2:
                        p.wait(g, "Fmm", i - 1)
                    p.inc(g.dma_start(
                        out=wsl[:, i % 2].rearrange("p (kc e) -> p kc e", e=128),
                        in_=w1c[fc].rearrange("(kc p) e -> p kc e", p=128)), "Fwl", 16)

                @block.tensor
                def _(t_, t=t, fc=fc, i=i):
                    p.wait(t_, "Fwl", 16 * (i + 1))
                    if i >= 2:
                        p.wait(t_, "Fge", i - 1)
                    if t >= 1 and fc == 0:
                        p.wait(t_, "Hod", 16 * 16 * t)
                    sl = wsl[:, i % 2]
                    for kc in range(KC):
                        t_.ldweights(sl[:, kc * 128:(kc + 1) * 128])
                        mmi = t_.matmul(psb[i % 2][:], sl[:, kc * 128:(kc + 1) * 128],
                                        x3_sb[:, kc * D + t * 512: kc * D + (t + 1) * 512],
                                        start=(kc == 0), stop=(kc == KC - 1))
                    p.inc(mmi, "Fmm", 1)

            for fo in range(EC):
                j = t * EC + fo

                @block.gpsimd
                def _(g, t=t, fo=fo, j=j):
                    if j >= 2:
                        p.wait(g, "Hmm", j - 1)
                    p.inc(g.dma_start(
                        out=w2sl[:, j % 2].rearrange("p (kf e) -> p kf e", e=128),
                        in_=w2c[fo].rearrange("(kf p) e -> p kf e", p=128)), "Hwl", 16)

                @block.tensor
                def _(t_, t=t, fo=fo, j=j):
                    p.wait(t_, "Hwl", 16 * (j + 1))
                    p.wait(t_, "Fge", 64 * (t + 1))
                    if j >= 2:
                        p.wait(t_, "Hev", j - 1)
                    sl = w2sl[:, j % 2]
                    for kf in range(FC):
                        t_.ldweights(sl[:, kf * 128:(kf + 1) * 128])
                        mmi = t_.matmul(psb[j % 2][:], sl[:, kf * 128:(kf + 1) * 128],
                                        A_sb[:, kf * 512:(kf + 1) * 512],
                                        start=(kf == 0), stop=(kf == FC - 1))
                    p.inc(mmi, "Hmm", 1)

                @block.sync
                def _(sy, t=t, fo=fo, j=j):
                    p.wait(sy, "Hev", j + 1)
                    p.inc(sy.dma_start(out=hdram[fo * 128:(fo + 1) * 128,
                                                 t * 512:(t + 1) * 512],
                                       in_=htile[:, j % 2]), "Hod", 16)

        es_ffn.close()
        es_ln = ExitStack()
        r32 = es_ln.enter_context(T("r32", [128, EC * 512], F32))
        rbf = es_ln.enter_context(T("rbf", [128, EC * 512], BF16))
        rsqb = es_ln.enter_context(T("rsqb", [128, EC * 512], BF16))
        xold = es_ln.enter_context(T("xold", [128, 2, 512], F32))
        mrep = es_ln.enter_context(T("mrep", [128, 512], F32))
        vrep = es_ln.enter_context(T("vrep", [128, 512], F32))
        t32 = es_ln.enter_context(T("t32", [128, 512], F32))
        rstd = es_ln.enter_context(T("rstd", [128, 2, 512], F32))
        xnf = es_ln.enter_context(T("xnf", [128, 2, 512], F32))
        htmp = es_ln.enter_context(T("htmp", [128, 2, 512], F32))

        @block.scalar
        def _(se):
            for t in range(TT):
                for f in range(EC):
                    i = t * EC + f
                    p.wait(se, "Lev", i + 1)
                    if t >= 1 and f == 0:
                        p.wait(se, "Lst", t)
                    p.inc(se.activation(out=rsqb[:, f * 512:(f + 1) * 512],
                                        in_=r32[:, f * 512:(f + 1) * 512],
                                        func=AF.Square), "Lsq", 1)
                p.wait(se, "Llm", t + 1)
                p.inc(se.activation(out=t32[:], in_=vrep[:], func=AF.Ln, bias=epsc[:, 0:1]),
                      "sch2", 1)
                p.wait(se, "sch2", t + 1)
                p.inc(se.activation(out=rstd[:, t % 2], in_=t32[:], func=AF.Exp,
                                    scale=nhalf[:, 0:1]), "Lrs", 1)
        for t in range(TT):
            for f in range(EC):
                i = t * EC + f

                @block.sync
                def _(sy, t=t, f=f, i=i):
                    if i == 0:
                        p.wait(sy, "Hod", 16 * 64)
                        p.wait(sy, "Wrc", 4)
                    if i >= 2:
                        p.wait(sy, "Lev", i - 1)
                    p.inc(sy.dma_start(out=htile[:, i % 2],
                                       in_=hdram[f * 128:(f + 1) * 128,
                                                 t * 512:(t + 1) * 512]), "Lhl", 16)
                    p.inc(sy.dma_start(out=xold[:, i % 2],
                                       in_=x3T[f * 128:(f + 1) * 128,
                                               t * 512:(t + 1) * 512]), "Lxl", 16)

                @block.vector
                def _(v, t=t, f=f, i=i):
                    p.wait(v, "Lhl", 16 * (i + 1))
                    p.wait(v, "Lxl", 16 * (i + 1))
                    if f == 0 and t >= 1:
                        p.wait(v, "Lst", t)
                    p.inc(v.tensor_tensor(htmp[:, i % 2], htile[:, i % 2],
                                          wrep[:, t * 512:(t + 1) * 512], ALU.mult), "vch", 1)
                    p.wait(v, "vch")
                    p.inc(v.tensor_tensor(r32[:, f * 512:(f + 1) * 512], htmp[:, i % 2],
                                          xold[:, i % 2], ALU.add), "Lev", 1)
                    p.inc(v.tensor_copy(rbf[:, f * 512:(f + 1) * 512],
                                        r32[:, f * 512:(f + 1) * 512]), "Lca", 1)

                @block.tensor
                def _(t_, t=t, f=f, i=i):
                    p.wait(t_, "Lca", i + 1)
                    p.wait(t_, "Lsq", i + 1)
                    if f == 0 and t >= 1:
                        p.wait(t_, "Llm", t)
                    if f == 0 and t == 0:
                        p.wait(t_, "Hmm", 64)
                    t_.ldweights(onesb[:])
                    t_.matmul(psM[:], onesb[:], rbf[:, f * 512:(f + 1) * 512],
                              start=(f == 0), stop=(f == EC - 1))
                    mmi = t_.matmul(psV[:], onesb[:], rsqb[:, f * 512:(f + 1) * 512],
                                    start=(f == 0), stop=(f == EC - 1))
                    if f == EC - 1:
                        p.inc(mmi, "Lst", 1)

            @block.vector
            def _(v, t=t):
                p.wait(v, "Lst", t + 1)
                p.inc(v.tensor_scalar(mrep[:], psM[:], invd[:, 0:1], None, ALU.mult), "vch", 1)
                v.tensor_scalar(vrep[:], psV[:], invd[:, 0:1], None, ALU.mult)
                p.wait(v, "vch")
                p.inc(v.tensor_mul(t32[:], mrep[:], mrep[:]), "vch", 1)
                p.wait(v, "vch")
                p.inc(v.tensor_tensor(vrep[:], vrep[:], t32[:], ALU.subtract), "Llm", 1)

            for f in range(EC):
                i = t * EC + f

                @block.vector
                def _(v, t=t, f=f, i=i):
                    if f == 0:
                        p.wait(v, "Lrs", t + 1)
                    if i >= 2:
                        p.wait(v, "Lol", 16 * (i - 1))
                    v.tensor_tensor(xnf[:, i % 2], r32[:, f * 512:(f + 1) * 512],
                                    mrep[:], ALU.subtract)
                    p.inc(v.tensor_tensor(xnf[:, i % 2], xnf[:, i % 2], rstd[:, t % 2],
                                          ALU.mult), "Lap", 1)

                @block.sync
                def _(sy, t=t, f=f, i=i):
                    p.wait(sy, "Lap", i + 1)
                    p.inc(sy.dma_start(out=outT[f * 128:(f + 1) * 128, t * 512:(t + 1) * 512],
                                       in_=xnf[:, i % 2]), "Lol", 16)

        @block.sync
        def _(sy):
            p.wait(sy, "Lol", 16 * 64)
    return nc



_BUILT = {}

def _get(name, fn):
    if name not in _BUILT:
        _BUILT[name] = fn()
    return _BUILT[name]


def kernel(**inputs):
    inp = inputs
    x = np.asarray(inp["L0_attn_x"], dtype=np.float32)
    m01 = m01_host()
    wq_s = host_chunk_w(np.asarray(inp["L0_attn_Wq"], np.float32).T)
    wk_s = host_chunk_w(np.asarray(inp["L0_attn_Wk"], np.float32).T)
    wv_t = np.ascontiguousarray(np.asarray(inp["L0_attn_Wv"], np.float32).T)
    cores = list(range(8))

    nc1 = _get("l1", build_launch1)
    in1 = []
    for c in range(8):
        b, half = c // 2, c % 2
        xT = np.ascontiguousarray(x[b, half * 2048:(half + 1) * 2048, :].T)
        in1.append({"xT": xT, "wq": wq_s, "wk": wk_s, "wv": wv_t, "m01": m01})
    r1 = run_bass_kernel_spmd(nc1, in1, core_ids=cores).results

    nc2 = _get("l2", build_launch2)
    w2 = prep_launch2_weights(inp)
    ctx0 = assemble_ctx0(r1)
    in2 = [{"ctxT": ctx0[c], "xT": in1[c]["xT"], **w2} for c in range(8)]
    r2 = run_bass_kernel_spmd(nc2, in2, core_ids=cores).results

    nc3 = _get("l3", build_launch3)
    in3 = [{**assemble_l1_attn_inputs(r2, c), "m01": m01} for c in range(8)]
    r3 = run_bass_kernel_spmd(nc3, in3, core_ids=cores).results

    nc4 = _get("l4", build_launch4)
    w4 = prep_launch4_weights(inp)
    in4 = [{"ctxT": assemble_l1_ctx(r3, c), "x2T": r2[c]["x2T"], **w4} for c in range(8)]
    r4 = run_bass_kernel_spmd(nc4, in4, core_ids=cores).results

    nc5 = _get("l5", build_launch5)
    w5 = prep_launch5_weights(inp)
    G = counts_matrix(r4)
    in5 = [{"x3T": r4[c]["x3T"], "x3bT": r4[c]["x3bT"], "G": G, "wsel": wsel_for_core(c), **w5}
           for c in range(8)]
    r5 = run_bass_kernel_spmd(nc5, in5, core_ids=cores).results

    out = np.zeros((4, 4096, 2048), np.float32)
    for c in range(8):
        b, half = c // 2, c % 2
        out[b, half * 2048:(half + 1) * 2048, :] = np.asarray(r5[c]["outT"], np.float32).T
    return out
